# version 02
# baseline (speedup 1.0000x reference)
import numpy as np
import concourse.bass as bass
import concourse.mybir as mybir

F32 = mybir.dt.float32
BF16 = mybir.dt.bfloat16
I32 = mybir.dt.int32
ALU = mybir.AluOpType
AF = mybir.ActivationFunctionType
AX = mybir.AxisListType

ENGINES = ("sync", "vector", "scalar", "tensor", "gpsimd")
C = 2048
TT = 2048
NT = TT // 128
NC = C // 128


def ts(i, n=128):
    return slice(i * n, (i + 1) * n)


class Prog:
    def __init__(self, nc):
        self.nc = nc
        self.steps = {e: [] for e in ENGINES}
        self.count = {e: 0 for e in ENGINES}
        self.last = None

    def step(self, engine, fn, deps=(), dma=False, tag="", raw=False):
        inc = 0 if raw else (16 if dma else 1)
        waits = [d for d in deps if d is not None]
        self.count[engine] += inc
        tok = (engine, self.count[engine])
        self.steps[engine].append((waits, fn, inc, tag))
        self.last = tok
        return tok

    def s(self, engine, fn, extra=(), dma=False, tag="", raw=False):
        return self.step(engine, fn, [self.last] + list(extra), dma=dma, tag=tag, raw=raw)

    def sdma(self, out, in_, engine="sync", extra=(), tag=""):
        return self.s(engine, lambda e: e.dma_start(out=out, in_=in_), extra, dma=True, tag=tag)

    def build(self):
        nc = self.nc
        sem_cms = {e: nc.semaphore("psem_" + e) for e in ENGINES}
        self.sems = {e: cm.__enter__() for e, cm in sem_cms.items()}
        block_cm = nc.Block()
        block = block_cm.__enter__()
        sems = self.sems

        def mk(engine):
            steps = self.steps[engine]

            def body(eng):
                waited = {}
                for waits, fn, inc, tag in steps:
                    for (de, dc) in waits:
                        if waited.get(de, 0) >= dc:
                            continue
                        eng.wait_ge(sems[de], dc)
                        waited[de] = max(waited.get(de, 0), dc)
                    inst = fn(eng)
                    if inc:
                        inst.then_inc(sems[engine], inc)
            return body

        for e in ENGINES:
            if self.steps[e]:
                getattr(block, e)(mk(e))
        block_cm.__exit__(None, None, None)
        for cm in sem_cms.values():
            cm.__exit__(None, None, None)


class Ctx:
    def __init__(self, nc):
        self.nc = nc
        self._cms = []
        self._scopes = []
        self.p = Prog(nc)

    def sbuf(self, name, shape, dtype):
        cm = self.nc.sbuf_tensor(name, shape, dtype)
        t = cm.__enter__()
        (self._scopes[-1] if self._scopes else self._cms).append(cm)
        return t

    def psum(self, name, shape, dtype):
        cm = self.nc.psum_tensor(name, shape, dtype)
        t = cm.__enter__()
        (self._scopes[-1] if self._scopes else self._cms).append(cm)
        return t

    def push_scope(self):
        self._scopes.append([])

    def pop_scope(self):
        for cm in reversed(self._scopes.pop()):
            cm.__exit__(None, None, None)

    def close(self):
        self.p.s("gpsimd", lambda e: e.engine_nop(), tag="final")
        self.p.build()
        for cm in reversed(self._cms):
            cm.__exit__(None, None, None)
        return self.nc


class Common:
    def __init__(self, cx, ident_ap):
        p = cx.p
        self.identf = cx.sbuf("cm_identf", [128, 128], F32)
        self.idb = cx.sbuf("cm_idb", [128, 128], BF16)
        p.sdma(self.identf[:], ident_ap)
        p.s("vector", lambda e: e.tensor_copy(self.idb[:], self.identf[:]))


def emit_ln_stats_tile(p, xs, sq, st, Cw=C):
    p.s("vector", lambda e: e.reduce_sum(st[:, 0:1], xs[:], axis=AX.X))
    p.s("scalar", lambda e: e.activation(sq[:], xs[:], AF.Square, accum_out=st[:, 1:2]))
    p.s("vector", lambda e: e.tensor_scalar(st[:, 2:3], st[:, 0:1], 1.0 / Cw, None, ALU.mult))
    p.s("vector", lambda e: e.tensor_scalar(st[:, 3:4], st[:, 1:2], 1.0 / Cw, None, ALU.mult))
    p.s("vector", lambda e: e.tensor_tensor(st[:, 4:5], st[:, 2:3], st[:, 2:3], ALU.mult))
    p.s("vector", lambda e: e.tensor_tensor(st[:, 5:6], st[:, 3:4], st[:, 4:5], ALU.subtract))
    p.s("vector", lambda e: e.tensor_scalar(st[:, 5:6], st[:, 5:6], 1e-5, None, ALU.add))
    p.s("scalar", lambda e: e.activation(st[:, 6:7], st[:, 5:6], AF.Ln))
    p.s("scalar", lambda e: e.activation(st[:, 7:8], st[:, 6:7], AF.Exp, scale=-0.5))


def emit_ln(cx, pfx, x_t, o_t, w_ap=None, b_ap=None, r_ts=(), halo_out=None):
    p = cx.p
    cx.push_scope()
    xt = x_t.rearrange("(n p) c -> n p c", p=128)
    rts = [r.rearrange("(n p) c -> n p c", p=128) for r in r_ts]
    ot = o_t.rearrange("(n p) c -> n p c", p=128)
    xs = cx.sbuf(pfx + "xs", [128, C], F32)
    rsb = [cx.sbuf(pfx + f"rs{j}", [128, C], F32) for j in range(len(r_ts))]
    osb = cx.sbuf(pfx + "osb", [128, C], F32)
    sq = cx.sbuf(pfx + "sq", [128, C], F32)
    st = cx.sbuf(pfx + "st", [128, 8], F32)
    if w_ap is not None:
        wt = cx.sbuf(pfx + "wt", [128, C], F32)
        bt = cx.sbuf(pfx + "bt", [128, C], F32)
        p.sdma(wt[:], w_ap)
        p.sdma(bt[:], b_ap)
    for i in range(NT):
        p.sdma(xs[:], xt[i])
        for ri in range(len(r_ts)):
            p.sdma(rsb[ri][:], rts[ri][i])
            p.s("vector", lambda e, ri=ri: e.tensor_tensor(xs[:], xs[:], rsb[ri][:], ALU.add))
        emit_ln_stats_tile(p, xs, sq, st)
        p.s("vector", lambda e: e.tensor_scalar(osb[:], xs[:], st[:, 2:3], st[:, 7:8], ALU.subtract, ALU.mult))
        if w_ap is not None:
            p.s("vector", lambda e: e.tensor_tensor(osb[:], osb[:], wt[:], ALU.mult))
            p.s("vector", lambda e: e.tensor_tensor(osb[:], osb[:], bt[:], ALU.add))
        p.sdma(ot[i], osb[:])
        if halo_out is not None:
            p.sdma(halo_out[i:i+1, :], osb[127:128, :])
    cx.pop_scope()


def emit_cc(cx, kind, in_ap, out_ap, ccsem, ccount, alu=None, groups8=False):
    p = cx.p
    op = alu if alu is not None else ALU.bypass
    groups = [[0, 1, 2, 3, 4, 5, 6, 7]] if groups8 else [[0, 1], [2, 3], [4, 5], [6, 7]]

    def fn(e):
        return e.collective_compute(kind, op, replica_groups=groups,
                                    ins=[in_ap.opt()], outs=[out_ap.opt()]).then_inc(ccsem, 1)
    p.s("gpsimd", fn, tag="cc", raw=True)
    ccount[0] += 1
    need = ccount[0]

    def fn2(e):
        e.wait_ge(ccsem, need)
        return e.engine_nop()
    p.s("gpsimd", fn2, tag="ccwait")


def emit_shift_tile(cx, pfx_bufs, xsrc_dram, halo_ap, i, dst_sb, svec, zmask, hrow):
    p = cx.p
    p.sdma(dst_sb[1:128, :], xsrc_dram[i * 128: i * 128 + 127, :])
    jm1 = max(i - 1, 0)
    p.sdma(hrow[0:1, 0:C], halo_ap[0:1, i, :])
    p.sdma(hrow[0:1, C:2*C], halo_ap[1:2, jm1, :])
    p.s("vector", lambda e: e.tensor_tensor(hrow[0:1, C:2*C], hrow[0:1, C:2*C], hrow[0:1, 0:C], ALU.subtract))
    p.s("vector", lambda e: e.scalar_tensor_tensor(dst_sb[0:1, :], hrow[0:1, C:2*C], svec[0:1, 0:1], hrow[0:1, 0:C], ALU.mult, ALU.add))
    p.s("vector", lambda e, i=i: e.tensor_scalar(dst_sb[0:1, :], dst_sb[0:1, :], zmask[0:1, i:i+1], None, ALU.mult))


def emit_mix_kvr(cx, cm, aps, xn_dram, halo_ap, svec, zmask, k_out, v_out, sr_out):
    p = cx.p
    cx.push_scope()
    TCH = 512
    NCH = TT // TCH
    xn_t = xn_dram.rearrange("(n p) c -> n p c", p=128)
    xnb = cx.sbuf("mk_xnb", [128, C], BF16)
    ldf = cx.sbuf("mk_ldf", [128, C], F32)
    hrow = cx.sbuf("mk_hrow", [1, 2 * C], F32)
    psT = cx.psum("mk_psT", [128, 512], BF16)
    xn_c = cx.sbuf("mk_xn_c", [128, NC, TCH], BF16)
    xx_c = cx.sbuf("mk_xx_c", [128, NC, TCH], BF16)
    mixed = cx.sbuf("mk_mixed", [128, NC, TCH], BF16)
    tmv_s = cx.sbuf("mk_tmv", [128, 3, NC], F32)
    wstrip = cx.sbuf("mk_ws", [128, NC, 128], F32)
    wstripb = cx.sbuf("mk_wsb", [128, NC, 128], BF16)
    psM = cx.psum("mk_psM", [128, TCH], F32)
    ev = cx.sbuf("mk_ev", [128, TCH], F32)

    for wi, tm in enumerate(["tmk", "tmv", "tmr"]):
        p.sdma(tmv_s[:, wi, :], aps[tm])
    for ch in range(NCH):
        for which, dst in [(0, xn_c), (1, xx_c)]:
            for tti in range(TCH // 128):
                i = ch * (TCH // 128) + tti
                if which == 0:
                    p.sdma(ldf[:], xn_t[i])
                else:
                    emit_shift_tile(cx, None, xn_dram, halo_ap, i, ldf, svec, zmask, hrow)
                p.s("vector", lambda e: e.tensor_copy(xnb[:], ldf[:]))
                for c4 in range(NC // 4):
                    for cc in range(4):
                        c = c4 * 4 + cc
                        p.s("tensor", lambda e, c=c, cc=cc: e.transpose(psT[:, ts(cc)], xnb[:, ts(c)], cm.idb[:]))
                    p.s("scalar", lambda e, c4=c4, dst=dst, tti=tti: e.activation(
                        dst[:, c4*4:(c4+1)*4, tti*128:(tti+1)*128],
                        psT[:].rearrange("p (a b) -> p a b", a=4), AF.Copy))
        for wi, (wn, outd, sig) in enumerate([("WkT", k_out, False), ("WvT", v_out, False), ("WrT", sr_out, True)]):
            for c in range(NC):
                p.s("vector", lambda e, c=c: e.tensor_tensor(mixed[:, c, :], xn_c[:, c, :], xx_c[:, c, :], ALU.subtract))
                p.s("vector", lambda e, c=c, wi=wi: e.scalar_tensor_tensor(mixed[:, c, :], mixed[:, c, :], tmv_s[:, wi, c:c+1], xx_c[:, c, :], ALU.mult, ALU.add))
            for dt in range(NC):
                p.sdma(wstrip[:], aps[wn].rearrange("(k p) d -> p k d", p=128)[:, :, ts(dt)])
                p.s("vector", lambda e: e.tensor_copy(wstripb[:], wstrip[:]))
                for kt2 in range(NC):
                    p.s("tensor", lambda e, kt2=kt2: e.matmul(psM[:], wstripb[:, kt2, :], mixed[:, kt2, :], start=(kt2 == 0), stop=(kt2 == NC - 1)))
                if sig:
                    p.s("scalar", lambda e: e.activation(ev[:], psM[:], AF.Sigmoid))
                else:
                    p.s("scalar", lambda e: e.activation(ev[:], psM[:], AF.Copy))
                p.sdma(outd.rearrange("(n p) t -> n p t", p=128)[dt, :, ch*TCH:(ch+1)*TCH], ev[:])
    cx.pop_scope()


def emit_wkv(cx, aps, kga, vga, y_out, svec_sb):
    p = cx.p
    cx.push_scope()
    TF = 4096
    NCT = 8
    kt = kga.rearrange("(two half ct p) (j i) -> half ct p j two i", two=2, half=2, p=128, i=128)
    vt = vga.rearrange("(two half ct p) (j i) -> half ct p j two i", two=2, half=2, p=128, i=128)
    yo = y_out.rearrange("(ct p) t -> ct p t", p=128)
    ks = cx.sbuf("wk_ks", [128, TF], F32)
    vs = cx.sbuf("wk_vs", [128, TF], F32)
    kb2 = cx.sbuf("wk_kb2", [128, TF], F32)
    es = cx.sbuf("wk_es", [128, TF], F32)
    zs = cx.sbuf("wk_zs", [128, TF], F32)
    rz = cx.sbuf("wk_rz", [128, TF + 1], F32)
    rr = cx.sbuf("wk_rr", [128, TF + 1], F32)
    num = cx.sbuf("wk_num", [128, TF], F32)
    den = cx.sbuf("wk_den", [128, TF], F32)
    tdb = cx.sbuf("wk_tdb", [128, NCT], F32)
    tfb = cx.sbuf("wk_tfb", [128, NCT], F32)
    lam = cx.sbuf("wk_lam", [128, NCT], F32)
    eu = cx.sbuf("wk_eu", [128, NCT], F32)
    sc = cx.sbuf("wk_sc", [128, 2], F32)

    p.sdma(tdb[:], aps["td8"])
    p.sdma(tfb[:], aps["tf8"])
    p.s("scalar", lambda e: e.activation(lam[:], tdb[:], AF.Exp))
    p.s("scalar", lambda e: e.activation(lam[:], lam[:], AF.Exp, scale=-1.0))
    p.s("scalar", lambda e: e.activation(eu[:], tfb[:], AF.Exp))
    for c in range(NCT):
        for src_t, dst in [(kt, ks), (vt, vs)]:
            p.sdma(kb2.ap().rearrange("p (j two i) -> p j two i", two=2, i=128), src_t[0, c])
            p.sdma(dst.ap().rearrange("p (j two i) -> p j two i", two=2, i=128), src_t[1, c])
            p.s("vector", lambda e, dst=dst: e.tensor_tensor(dst[:], dst[:], kb2[:], ALU.subtract))
            p.s("vector", lambda e, dst=dst: e.scalar_tensor_tensor(dst[:], dst[:], svec_sb, kb2[:], ALU.mult, ALU.add))
        p.s("vector", lambda e: e.reduce_max(sc[:, 0:1], ks[:], axis=AX.X))
        p.s("vector", lambda e: e.tensor_scalar(sc[:, 1:2], sc[:, 0:1], -1.0, None, ALU.mult))
        p.s("scalar", lambda e: e.activation(es[:], ks[:], AF.Exp, bias=sc[:, 1:2]))
        p.s("vector", lambda e: e.tensor_tensor(zs[:], es[:], vs[:], ALU.mult))
        p.s("vector", lambda e: e.memset(rz[:, 0:1], 0.0))
        p.s("vector", lambda e: e.memset(rr[:, 0:1], 0.0))
        p.s("vector", lambda e, c=c: e.tensor_tensor_scan(
            rz[:, 1:TF + 1], lam[:, c:c + 1].broadcast_to([128, TF]), zs[:], 0.0, ALU.mult, ALU.add))
        p.s("vector", lambda e, c=c: e.tensor_tensor_scan(
            rr[:, 1:TF + 1], lam[:, c:c + 1].broadcast_to([128, TF]), es[:], 0.0, ALU.mult, ALU.add))
        p.s("vector", lambda e, c=c: e.scalar_tensor_tensor(num[:], zs[:], eu[:, c:c + 1], rz[:, 0:TF], ALU.mult, ALU.add))
        p.s("vector", lambda e, c=c: e.scalar_tensor_tensor(den[:], es[:], eu[:, c:c + 1], rr[:, 0:TF], ALU.mult, ALU.add))
        p.s("scalar", lambda e: e.activation(den[:], den[:], AF.Ln))
        p.s("scalar", lambda e: e.activation(den[:], den[:], AF.Exp, scale=-1.0))
        p.s("vector", lambda e: e.tensor_tensor(num[:], num[:], den[:], ALU.mult))
        p.sdma(yo[c], num[:])
    cx.pop_scope()


def emit_wo_res(cx, cm, wo_ap, att_ap, sr_ap, x_ap, x2_ap, use_sr=True, D=C,
                att_is_gather=False, svec_sb=None, pfx="wo_"):
    p = cx.p
    cx.push_scope()
    if att_is_gather:
        att_g = att_ap.rearrange("(n p) (j two i) -> two p n j i", p=128, two=2, i=128)
    else:
        att_r = att_ap.rearrange("(n p) t -> p n t", p=128)
    sr_r = sr_ap.rearrange("(n p) t -> p n t", p=128) if use_sr else None
    xt = x_ap.rearrange("(n p) c -> n p c", p=128)
    x2t = x2_ap.rearrange("(n p) c -> n p c", p=128)
    wof = cx.sbuf(pfx + "wof", [128, D], F32)
    wob = cx.sbuf(pfx + "wob", [128, NC, D], BF16)
    av = cx.sbuf(pfx + "av", [128, NC, 128], F32)
    av2 = cx.sbuf(pfx + "av2", [128, NC, 128], F32)
    sv = cx.sbuf(pfx + "sv", [128, NC, 128], F32)
    sab = cx.sbuf(pfx + "sab", [128, NC, 128], BF16)
    xs = cx.sbuf(pfx + "xs", [128, D], F32)
    ev = cx.sbuf(pfx + "ev", [128, D], F32)
    ps = cx.psum(pfx + "ps", [128, D], F32)

    for kt2 in range(NC):
        p.sdma(wof[:], wo_ap.rearrange("(n p) d -> n p d", p=128)[kt2])
        p.s("vector", lambda e, kt2=kt2: e.tensor_copy(wob[:, kt2, :], wof[:]))
    for i in range(NT):
        if att_is_gather:
            p.sdma(av2[:], att_g[0, :, :, i, :])
            p.sdma(av[:], att_g[1, :, :, i, :])
            p.s("vector", lambda e: e.tensor_tensor(av[:], av[:], av2[:], ALU.subtract))
            p.s("vector", lambda e: e.scalar_tensor_tensor(av[:], av[:], svec_sb, av2[:], ALU.mult, ALU.add))
        else:
            p.sdma(av[:], att_r[:, :, ts(i)])
        if use_sr:
            p.sdma(sv[:], sr_r[:, :, ts(i)])
            p.s("vector", lambda e: e.tensor_tensor(sab[:], av[:], sv[:], ALU.mult))
        else:
            p.s("vector", lambda e: e.tensor_copy(sab[:], av[:]))
        for kt2 in range(NC):
            for dh in range(D // 512):
                p.s("tensor", lambda e, kt2=kt2, dh=dh: e.matmul(
                    ps[:, dh*512:(dh+1)*512], sab[:, kt2, :], wob[:, kt2, dh*512:(dh+1)*512],
                    start=(kt2 == 0), stop=(kt2 == NC - 1)))
        p.sdma(xs[:], xt[i])
        p.s("vector", lambda e: e.tensor_tensor(ev[:], ps[:], xs[:], ALU.add))
        p.sdma(x2t[i], ev[:])
    cx.pop_scope()


def emit_cmix(cx, cm, aps, xn2_dram, halo_ap, svec, zmask, rkvt_out, H=8192):
    p = cx.p
    cx.push_scope()
    TCH = 512
    NCH = TT // TCH
    NH = H // 128
    NSLAB = 4
    SL = NH // NSLAB
    ldf = cx.sbuf("cmx_ldf", [128, C], F32)
    bfa = cx.sbuf("cmx_bfa", [128, C], BF16)
    hrow = cx.sbuf("cmx_hrow", [1, 2 * C], F32)
    psT = cx.psum("cmx_psT", [128, 512], BF16)
    A = cx.sbuf("cmx_A", [128, NC, TCH], BF16)
    Bb = cx.sbuf("cmx_B", [128, NC, TCH], BF16)
    ck = cx.sbuf("cmx_ck", [128, NC, TCH], BF16)
    cr = cx.sbuf("cmx_cr", [128, NC, TCH], BF16)
    cmks = cx.sbuf("cmx_cmks", [128, 2, NC], F32)
    wstrip = cx.sbuf("cmx_ws", [128, NC, 128], F32)
    wstripb = cx.sbuf("cmx_wsb", [128, NC, 128], BF16)
    hslab = cx.sbuf("cmx_hslab", [128, SL, TCH], BF16)
    kvacc = cx.sbuf("cmx_kvacc", [128, NC, TCH], F32)
    psM = cx.psum("cmx_psM", [128, TCH], F32)
    ev = cx.sbuf("cmx_ev", [128, TCH], F32)
    sg = cx.sbuf("cmx_sg", [128, TCH], F32)
    psTT = cx.psum("cmx_psTT", [128, TCH], F32)
    evT = cx.sbuf("cmx_evT", [128, TCH], F32)
    rk_t = rkvt_out.rearrange("(n p) c -> n p c", p=128)

    p.sdma(cmks[:, 0, :], aps["cmk"])
    p.sdma(cmks[:, 1, :], aps["cmr"])
    for ch in range(NCH):
        for which, dst in [(0, A), (1, Bb)]:
            for tti in range(TCH // 128):
                i = ch * (TCH // 128) + tti
                if which == 0:
                    p.sdma(ldf[:], xn2_dram[i*128:(i+1)*128, :])
                else:
                    emit_shift_tile(cx, None, xn2_dram, halo_ap, i, ldf, svec, zmask, hrow)
                p.s("vector", lambda e: e.tensor_copy(bfa[:], ldf[:]))
                for c4 in range(NC // 4):
                    for cc in range(4):
                        c = c4 * 4 + cc
                        p.s("tensor", lambda e, c=c, cc=cc: e.transpose(psT[:, ts(cc)], bfa[:, ts(c)], cm.idb[:]))
                    p.s("scalar", lambda e, c4=c4, dst=dst, tti=tti: e.activation(
                        dst[:, c4*4:(c4+1)*4, tti*128:(tti+1)*128],
                        psT[:].rearrange("p (a b) -> p a b", a=4), AF.Copy))
        for c in range(NC):
            p.s("vector", lambda e, c=c: e.tensor_tensor(cr[:, c, :], A[:, c, :], Bb[:, c, :], ALU.subtract))
            p.s("vector", lambda e, c=c: e.scalar_tensor_tensor(ck[:, c, :], cr[:, c, :], cmks[:, 0, c:c+1], Bb[:, c, :], ALU.mult, ALU.add))
            p.s("vector", lambda e, c=c: e.scalar_tensor_tensor(cr[:, c, :], cr[:, c, :], cmks[:, 1, c:c+1], Bb[:, c, :], ALU.mult, ALU.add))
        p.s("vector", lambda e: e.memset(kvacc[:], 0.0))
        for sl in range(NSLAB):
            for hj in range(SL):
                ht = sl * SL + hj
                p.sdma(wstrip[:], aps["WckT"].rearrange("(n p) h -> p n h", p=128)[:, :, ts(ht)])
                p.s("vector", lambda e: e.tensor_copy(wstripb[:], wstrip[:]))
                for kt2 in range(NC):
                    p.s("tensor", lambda e, kt2=kt2: e.matmul(psM[:], wstripb[:, kt2, :], ck[:, kt2, :], start=(kt2 == 0), stop=(kt2 == NC - 1)))
                p.s("vector", lambda e: e.tensor_scalar_max(ev[:], psM[:], 0.0))
                p.s("scalar", lambda e, hj=hj: e.activation(hslab[:, hj, :], ev[:], AF.Square))
            for ct in range(NC):
                p.sdma(wstrip[:, 0:SL, :], aps["WcvT"].rearrange("(n p) c -> p n c", p=128)[:, sl*SL:(sl+1)*SL, ts(ct)])
                p.s("vector", lambda e: e.tensor_copy(wstripb[:, 0:SL, :], wstrip[:, 0:SL, :]))
                for hj in range(SL):
                    p.s("tensor", lambda e, hj=hj: e.matmul(psM[:], wstripb[:, hj, :], hslab[:, hj, :], start=(hj == 0), stop=(hj == SL - 1)))
                p.s("vector", lambda e, ct=ct: e.tensor_tensor(kvacc[:, ct, :], kvacc[:, ct, :], psM[:], ALU.add))
        for ct in range(NC):
            p.sdma(wstrip[:], aps["WcrT"].rearrange("(n p) d -> p n d", p=128)[:, :, ts(ct)])
            p.s("vector", lambda e: e.tensor_copy(wstripb[:], wstrip[:]))
            for kt2 in range(NC):
                p.s("tensor", lambda e, kt2=kt2: e.matmul(psM[:], wstripb[:, kt2, :], cr[:, kt2, :], start=(kt2 == 0), stop=(kt2 == NC - 1)))
            p.s("scalar", lambda e: e.activation(sg[:], psM[:], AF.Sigmoid))
            p.s("vector", lambda e, ct=ct: e.tensor_tensor(ev[:], sg[:], kvacc[:, ct, :], ALU.mult))
            for tt in range(4):
                p.s("tensor", lambda e, tt=tt: e.transpose(psTT[:, ts(tt)], ev[:, ts(tt)], cm.identf[:]))
            p.s("scalar", lambda e: e.activation(evT[:], psTT[:], AF.Copy))
            for tt in range(4):
                i = ch * 4 + tt
                p.sdma(rk_t[i][:, ts(ct)], evT[:, ts(tt)])
    cx.pop_scope()


def emit_l0ffn(cx, cm, aps, x_dram, ffot_out, I=5632):
    p = cx.p
    cx.push_scope()
    NI = I // 128
    TCH = 512
    NCH = TT // TCH
    x_t = x_dram.rearrange("(n p) c -> n p c", p=128)
    ldf = cx.sbuf("ff_ldf", [128, C], F32)
    bfa = cx.sbuf("ff_bfa", [128, C], BF16)
    psT = cx.psum("ff_psT", [128, 512], BF16)
    xc = cx.sbuf("ff_xc", [128, NC, TCH], BF16)
    wstrip = cx.sbuf("ff_ws", [128, NC, 128], F32)
    wstripb = cx.sbuf("ff_wsb", [128, NC, 128], BF16)
    w2strip = cx.sbuf("ff_w2s", [128, NI, 128], F32)
    w2stripb = cx.sbuf("ff_w2sb", [128, NI, 128], BF16)
    hdn = cx.sbuf("ff_hdn", [128, NI, TCH], BF16)
    scl = cx.sbuf("ff_scl", [128, 3, max(NI, NC)], F32)
    psM = cx.psum("ff_psM", [128, TCH], F32)
    ev = cx.sbuf("ff_ev", [128, TCH], F32)
    ev2 = cx.sbuf("ff_ev2", [128, TCH], F32)
    psTT = cx.psum("ff_psTT", [128, TCH], F32)
    evT = cx.sbuf("ff_evT", [128, TCH], F32)
    ff_t = ffot_out.rearrange("(n p) c -> n p c", p=128)

    p.sdma(scl[:, 0, 0:NI], aps["gs"])
    p.sdma(scl[:, 1, 0:NI], aps["us"])
    p.sdma(scl[:, 2, 0:NC], aps["ds"])
    for ch in range(NCH):
        for tti in range(TCH // 128):
            i = ch * (TCH // 128) + tti
            p.sdma(ldf[:], x_t[i])
            p.s("vector", lambda e: e.tensor_copy(bfa[:], ldf[:]))
            for c4 in range(NC // 4):
                for cc in range(4):
                    c = c4 * 4 + cc
                    p.s("tensor", lambda e, c=c, cc=cc: e.transpose(psT[:, ts(cc)], bfa[:, ts(c)], cm.idb[:]))
                p.s("scalar", lambda e, c4=c4, tti=tti: e.activation(
                    xc[:, c4*4:(c4+1)*4, tti*128:(tti+1)*128],
                    psT[:].rearrange("p (a b) -> p a b", a=4), AF.Copy))
        for it in range(NI):
            p.sdma(wstrip[:], aps["gateT"].rearrange("(n p) i -> p n i", p=128)[:, :, ts(it)])
            p.s("vector", lambda e: e.tensor_copy(wstripb[:], wstrip[:]))
            for kt2 in range(NC):
                p.s("tensor", lambda e, kt2=kt2: e.matmul(psM[:], wstripb[:, kt2, :], xc[:, kt2, :], start=(kt2 == 0), stop=(kt2 == NC - 1)))
            p.s("vector", lambda e, it=it: e.tensor_scalar(ev[:], psM[:], scl[:, 0, it:it+1], None, ALU.mult))
            p.s("scalar", lambda e: e.activation(ev[:], ev[:], AF.Silu))
            p.sdma(wstrip[:], aps["upT"].rearrange("(n p) i -> p n i", p=128)[:, :, ts(it)])
            p.s("vector", lambda e: e.tensor_copy(wstripb[:], wstrip[:]))
            for kt2 in range(NC):
                p.s("tensor", lambda e, kt2=kt2: e.matmul(psM[:], wstripb[:, kt2, :], xc[:, kt2, :], start=(kt2 == 0), stop=(kt2 == NC - 1)))
            p.s("vector", lambda e, it=it: e.tensor_scalar(ev2[:], psM[:], scl[:, 1, it:it+1], None, ALU.mult))
            p.s("vector", lambda e, it=it: e.tensor_tensor(hdn[:, it, :], ev[:], ev2[:], ALU.mult))
        for ct in range(NC):
            p.sdma(w2strip[:], aps["downT"].rearrange("(n p) c -> p n c", p=128)[:, :, ts(ct)])
            p.s("vector", lambda e: e.tensor_copy(w2stripb[:], w2strip[:]))
            for kt2 in range(NI):
                p.s("tensor", lambda e, kt2=kt2: e.matmul(psM[:], w2stripb[:, kt2, :], hdn[:, kt2, :], start=(kt2 == 0), stop=(kt2 == NI - 1)))
            p.s("vector", lambda e, ct=ct: e.tensor_scalar(ev[:], psM[:], scl[:, 2, ct:ct+1], None, ALU.mult))
            for tt in range(4):
                p.s("tensor", lambda e, tt=tt: e.transpose(psTT[:, ts(tt)], ev[:, ts(tt)], cm.identf[:]))
            p.s("scalar", lambda e: e.activation(evT[:], psTT[:], AF.Copy))
            for tt in range(4):
                i = ch * 4 + tt
                p.sdma(ff_t[i][:, ts(ct)], evT[:, ts(tt)])
    cx.pop_scope()


def emit_proj3(cx, cm, aps, x_dram, q_out, kin_out, vin_out):
    p = cx.p
    cx.push_scope()
    TCH = 512
    NCH = TT // TCH
    x_t = x_dram.rearrange("(n p) c -> n p c", p=128)
    ldf = cx.sbuf("p3_ldf", [128, C], F32)
    bfa = cx.sbuf("p3_bfa", [128, C], BF16)
    psT = cx.psum("p3_psT", [128, 512], BF16)
    xc = cx.sbuf("p3_xc", [128, NC, TCH], BF16)
    wstrip = cx.sbuf("p3_ws", [128, NC, 128], F32)
    wstripb = cx.sbuf("p3_wsb", [128, NC, 128], BF16)
    psM = cx.psum("p3_psM", [128, TCH], F32)
    ev = cx.sbuf("p3_ev", [128, TCH], F32)

    for ch in range(NCH):
        for tti in range(TCH // 128):
            i = ch * (TCH // 128) + tti
            p.sdma(ldf[:], x_t[i])
            p.s("vector", lambda e: e.tensor_copy(bfa[:], ldf[:]))
            for c4 in range(NC // 4):
                for cc in range(4):
                    c = c4 * 4 + cc
                    p.s("tensor", lambda e, c=c, cc=cc: e.transpose(psT[:, ts(cc)], bfa[:, ts(c)], cm.idb[:]))
                p.s("scalar", lambda e, c4=c4, tti=tti: e.activation(
                    xc[:, c4*4:(c4+1)*4, tti*128:(tti+1)*128],
                    psT[:].rearrange("p (a b) -> p a b", a=4), AF.Copy))
        for wn, outd in [("Wq1", q_out), ("Wk1", kin_out), ("Wv1", vin_out)]:
            for dt in range(NC):
                p.sdma(wstrip[:], aps[wn].rearrange("(k p) d -> p k d", p=128)[:, :, ts(dt)])
                p.s("vector", lambda e: e.tensor_copy(wstripb[:], wstrip[:]))
                for kt2 in range(NC):
                    p.s("tensor", lambda e, kt2=kt2: e.matmul(psM[:], wstripb[:, kt2, :], xc[:, kt2, :], start=(kt2 == 0), stop=(kt2 == NC - 1)))
                p.s("scalar", lambda e: e.activation(ev[:], psM[:], AF.Copy))
                p.sdma(outd.rearrange("(n p) t -> n p t", p=128)[dt, :, ch*TCH:(ch+1)*TCH], ev[:])
    cx.pop_scope()


def emit_attn(cx, cm, aps, q_dram, kg_dram, vg_dram, o_out, TF=4096, HEADS=16, QG=4):
    p = cx.p
    cx.push_scope()
    NKB = TF // 128
    NQG = NT // QG
    W = QG * 128
    Qr = q_dram.rearrange("(h p) t -> h p t", p=128)
    Kr = kg_dram.rearrange("(two hh p) (j i) -> hh p j two i", two=2, p=128, i=128)
    Vr = vg_dram.rearrange("(two hh p) (j i) -> hh p j two i", two=2, p=128, i=128)
    Or = o_out.rearrange("(h p) t -> h p t", p=128)

    ldf = cx.sbuf("at_ldf", [128, TF], F32)
    vhb = cx.sbuf("at_vhb", [128, TF], BF16)
    vt = cx.sbuf("at_vt", [128, NKB, 128], BF16)
    khb = cx.sbuf("at_khb", [128, TF], BF16)
    qhb = cx.sbuf("at_qhb", [128, TT], BF16)
    m1s = cx.sbuf("at_m1s", [128, 8, W], F32)
    m2s = cx.sbuf("at_m2s", [128, 8, W], BF16)
    m2f = cx.sbuf("at_m2f", [128, 8, W], F32)
    selb = cx.sbuf("at_selb", [32, NT], BF16)
    self_ = cx.sbuf("at_self", [32, NT], F32)
    cnts = cx.sbuf("at_cnts", [1, NT], F32)
    onesk = cx.sbuf("at_onesk", [128, 1], BF16)
    onesr = cx.sbuf("at_onesr", [1, 128], BF16)
    bsum = cx.sbuf("at_bsum", [128, NKB], F32)
    bsumtb = cx.sbuf("at_bsumtb", [32, 128], BF16)
    sufq = cx.sbuf("at_sufq", [128, NT], F32)
    sufev = cx.sbuf("at_sufev", [16, 128], F32)
    pexp = cx.sbuf("at_pexp", [128, W], BF16)
    den_s = cx.sbuf("at_dens", [1, W], F32)
    denb = cx.sbuf("at_denb", [1, W], BF16)
    obuf = cx.sbuf("at_obuf", [128, W], F32)
    psS = cx.psum("at_psS", [128, W], F32)
    psO = cx.psum("at_psO", [128, W], F32)
    psD = cx.psum("at_psD", [1, W], F32)
    psT2 = cx.psum("at_psT2", [128, 128], BF16)
    psTf = cx.psum("at_psTf", [32, 128], F32)
    psSuf = cx.psum("at_psSuf", [16, 128], F32)
    psSufT = cx.psum("at_psSufT", [128, 16], F32)
    psB = cx.psum("at_psB", [128, W], F32)

    p.s("vector", lambda e: e.memset(onesk[:], 1.0))
    p.s("vector", lambda e: e.memset(onesr[:], 1.0))
    p.sdma(m1s[:], aps["m1"].rearrange("m p w -> p m w"))
    p.sdma(m2f[:], aps["m2"].rearrange("m p w -> p m w"))
    p.s("vector", lambda e: e.tensor_copy(m2s[:], m2f[:]))
    p.sdma(self_[:], aps["Sel"])
    p.s("vector", lambda e: e.tensor_copy(selb[:], self_[:]))
    p.sdma(cnts[:], aps["cnt"])
    ldf4 = ldf.ap().rearrange("p (j two i) -> p j two i", two=2, i=128)
    for h in range(HEADS):
        p.sdma(ldf4, Kr[h])
        p.s("vector", lambda e: e.tensor_copy(khb[:], ldf[:]))
        p.sdma(ldf[:, 0:TT], Qr[h])
        p.s("vector", lambda e: e.tensor_scalar(qhb[:], ldf[:, 0:TT], 1.0 / 64.0, None, ALU.mult))
        p.sdma(ldf4, Vr[h])
        p.s("vector", lambda e: e.tensor_copy(vhb[:], ldf[:]))
        p.s("vector", lambda e: e.reduce_sum(bsum[:], ldf.ap().rearrange("p (a b) -> p a b", a=NKB), axis=AX.X))
        for kb in range(NKB):
            p.s("tensor", lambda e, kb=kb: e.transpose(psT2[:], vhb[:, ts(kb)], cm.idb[:]))
            p.s("scalar", lambda e, kb=kb: e.activation(vt[:, kb, :], psT2[:], AF.Copy))
        p.s("tensor", lambda e: e.transpose(psTf[:], bsum[:], cm.identf[:]))
        p.s("vector", lambda e: e.tensor_copy(bsumtb[:], psTf[:]))
        p.s("tensor", lambda e: e.matmul(psSuf[:], selb[:], bsumtb[:], start=True, stop=True))
        p.s("scalar", lambda e: e.activation(sufev[:], psSuf[:], AF.Copy))
        p.s("tensor", lambda e: e.transpose(psSufT[:], sufev[:], cm.identf[0:16, 0:16]))
        p.s("scalar", lambda e: e.activation(sufq[:, 0:NT], psSufT[:], AF.Copy))
        for gi in range(NQG):
            qj0 = gi * QG
            qcols = slice(qj0 * 128, qj0 * 128 + W)
            nkb_c = 2 * qj0
            total_kb = nkb_c + 8
            for kb in range(total_kb):
                mi = kb - nkb_c
                p.s("tensor", lambda e, kb=kb, qcols=qcols: e.matmul(psS[:], khb[:, ts(kb)], qhb[:, qcols], start=True, stop=True))
                if mi >= 0:
                    p.s("vector", lambda e, mi=mi: e.tensor_tensor(psS[:], psS[:], m1s[:, mi, :], ALU.mult))
                p.s("scalar", lambda e: e.activation(pexp[:], psS[:], AF.Exp))
                if mi >= 0:
                    p.s("vector", lambda e, mi=mi: e.tensor_tensor(pexp[:], pexp[:], m2s[:, mi, :], ALU.mult))
                p.s("tensor", lambda e, kb=kb, total_kb=total_kb: e.matmul(psO[:], vt[:, kb, :], pexp[:], start=(kb == 0), stop=(kb == total_kb - 1), skip_group_check=True))
                p.s("tensor", lambda e, kb=kb, total_kb=total_kb: e.matmul(psD[:], onesk[:], pexp[:], start=(kb == 0), stop=(kb == total_kb - 1), skip_group_check=True))
            p.s("vector", lambda e: e.tensor_copy(den_s[:], psD[:]))
            for t in range(QG):
                p.s("vector", lambda e, t=t, qj0=qj0: e.tensor_scalar(den_s[:, ts(t)], den_s[:, ts(t)], cnts[:, qj0 + t: qj0 + t + 1], None, ALU.add))
            p.s("scalar", lambda e: e.activation(den_s[:], den_s[:], AF.Ln))
            p.s("scalar", lambda e: e.activation(den_s[:], den_s[:], AF.Exp, scale=-1.0))
            p.s("vector", lambda e: e.tensor_copy(denb[:], den_s[:]))
            p.s("tensor", lambda e: e.matmul(psB[:], onesr[:], denb[:], start=True, stop=True))
            for t in range(QG):
                p.s("vector", lambda e, t=t, qj0=qj0: e.tensor_scalar(obuf[:, ts(t)], psO[:, ts(t)], sufq[:, qj0 + t: qj0 + t + 1], None, ALU.add))
            p.s("vector", lambda e: e.tensor_tensor(obuf[:], obuf[:], psB[:], ALU.mult))
            p.sdma(Or[h][:, qcols], obuf[:])
    cx.pop_scope()


def emit_round_chain(p, x, s127, z, m, oi):
    p.s("vector", lambda e: e.tensor_scalar(z[:], x[:], s127, 128.5, ALU.mult, ALU.add))
    p.s("vector", lambda e: e.tensor_copy(oi[:], z[:]))
    p.s("vector", lambda e: e.tensor_copy(m[:], oi[:]))
    p.s("vector", lambda e: e.tensor_tensor(z[:], m[:], z[:], ALU.is_gt))
    p.s("vector", lambda e: e.tensor_tensor(m[:], m[:], z[:], ALU.subtract))


def emit_quant(p, out_bf, x, s127, z, m, oi):
    emit_round_chain(p, x, s127, z, m, oi)
    p.s("vector", lambda e: e.tensor_scalar(out_bf[:], m[:], -128.0, None, ALU.add))


def emit_wquant(p, out_bf, w, swv, z, m, oi):
    emit_round_chain(p, w, swv, z, m, oi)
    p.s("vector", lambda e: e.tensor_scalar(m[:], m[:], -128.0, None, ALU.add))
    p.s("vector", lambda e: e.tensor_scalar(out_bf[:], m[:], 1.0, -1.0, ALU.min, ALU.max))


def emit_bitffn(cx, cm, aps, x5_dram, h1_dram, w1t_dram, w2t_dram, x6_out, H=8192):
    emit_bitffn_p1(cx, cm, aps, x5_dram, h1_dram, w1t_dram, H=H)
    emit_bitffn_p2(cx, cm, aps, x5_dram, h1_dram, w2t_dram, x6_out, H=H)


def emit_bitffn_p1(cx, cm, aps, x5_dram, h1_dram, w1t_dram, H=8192):
    p = cx.p
    NH = H // 128
    cx.push_scope()
    x_t = x5_dram.rearrange("(n p) c -> n p c", p=128)
    w_r = aps["W1T"].rearrange("(n p) h -> n p h", p=128)
    wt_r = w1t_dram.rearrange("(n p) h -> n p h", p=128)
    h1_t = h1_dram.rearrange("(n p) h -> n p h", p=128)
    WCH = 2048
    ldw = cx.sbuf("b1_ldw", [128, WCH], F32)
    zb = cx.sbuf("b1_zb", [128, WCH], F32)
    mb = cx.sbuf("b1_mb", [128, WCH], F32)
    oib = cx.sbuf("b1_oib", [128, WCH], I32)
    wtb = cx.sbuf("b1_wtb", [128, WCH], BF16)
    acc = cx.sbuf("b1_acc", [128, 2], F32)
    onescol = cx.sbuf("b1_onesc", [128, 1], F32)
    onesrow = cx.sbuf("b1_onesr", [1, 128], F32)
    sw1 = cx.sbuf("b1_sw1", [1, 2], F32)
    swv = cx.sbuf("b1_swv", [128, 1], F32)
    mnv = cx.sbuf("b1_mnv", [128, 1], F32)
    fct = cx.sbuf("b1_fct", [128, NT], F32)
    psS1 = cx.psum("b1_psS1", [1, 1], F32)
    psRep = cx.psum("b1_psRep", [128, 1], F32)
    ldf = cx.sbuf("b1_ldf", [128, C], F32)
    xqb = cx.sbuf("b1_xqb", [128, C], BF16)
    xc = cx.sbuf("b1_xc", [128, NC, TT], BF16)
    st = cx.sbuf("b1_st", [128, 8], F32)
    psT = cx.psum("b1_psT", [128, 512], BF16)
    wblk = cx.sbuf("b1_wblk", [128, NC, 512], BF16)
    psM = cx.psum("b1_psM", [128, 512], F32)
    ev = cx.sbuf("b1_ev", [128, 512], F32)

    p.s("vector", lambda e: e.memset(onescol[:], 1.0))
    p.s("vector", lambda e: e.memset(onesrow[:], 1.0))
    p.s("vector", lambda e: e.memset(acc[:, 0:1], 0.0))
    for ct in range(NC):
        for hch in range(H // WCH):
            p.sdma(ldw[:], w_r[ct][:, hch*WCH:(hch+1)*WCH])
            p.s("vector", lambda e: e.reduce_sum(acc[:, 1:2], ldw[:], axis=AX.X, apply_absolute_value=True))
            p.s("vector", lambda e: e.tensor_tensor(acc[:, 0:1], acc[:, 0:1], acc[:, 1:2], ALU.add))
    p.s("tensor", lambda e: e.matmul(psS1[:], acc[:, 0:1], onescol[:], start=True, stop=True))
    p.s("vector", lambda e: e.tensor_scalar(sw1[:, 0:1], psS1[:], 1.0 / (C * H), 1e-5, ALU.mult, ALU.max))
    p.s("vector", lambda e: e.reciprocal(sw1[:, 1:2], sw1[:, 0:1]))
    p.s("tensor", lambda e: e.matmul(psRep[:], onesrow[:], sw1[:, 1:2], start=True, stop=True))
    p.s("scalar", lambda e: e.activation(swv[:], psRep[:], AF.Copy))
    p.s("tensor", lambda e: e.matmul(psRep[:], onesrow[:], sw1[:, 0:1], start=True, stop=True))
    p.s("scalar", lambda e: e.activation(mnv[:], psRep[:], AF.Copy))
    for ct in range(NC):
        for hch in range(H // WCH):
            p.sdma(ldw[:], w_r[ct][:, hch*WCH:(hch+1)*WCH])
            emit_wquant(p, wtb, ldw, swv[:, 0:1], zb, mb, oib)
            p.sdma(wt_r[ct][:, hch*WCH:(hch+1)*WCH], wtb[:])
    for i in range(NT):
        p.sdma(ldf[:], x_t[i])
        p.s("vector", lambda e: e.reduce_max(st[:, 0:1], ldf[:], axis=AX.X, apply_absolute_value=True))
        p.s("vector", lambda e: e.tensor_scalar(st[:, 0:1], st[:, 0:1], 1e-5, None, ALU.max))
        p.s("vector", lambda e: e.reciprocal(st[:, 1:2], st[:, 0:1]))
        p.s("vector", lambda e: e.tensor_scalar(st[:, 2:3], st[:, 1:2], 127.0, None, ALU.mult))
        emit_quant(p, xqb, ldf, st[:, 2:3], zb[:, 0:C], mb[:, 0:C], oib[:, 0:C])
        p.s("vector", lambda e, i=i: e.tensor_scalar(fct[:, i:i+1], st[:, 0:1], 1.0 / 127.0, None, ALU.mult))
        p.s("vector", lambda e, i=i: e.tensor_tensor(fct[:, i:i+1], fct[:, i:i+1], mnv[:], ALU.mult))
        for c4 in range(NC // 4):
            for cc in range(4):
                c = c4 * 4 + cc
                p.s("tensor", lambda e, c=c, cc=cc: e.transpose(psT[:, ts(cc)], xqb[:, ts(c)], cm.idb[:]))
            p.s("scalar", lambda e, c4=c4, i=i: e.activation(
                xc[:, c4*4:(c4+1)*4, i*128:(i+1)*128],
                psT[:].rearrange("p (a b) -> p a b", a=4), AF.Copy))
    for hcki in range(H // 512):
        p.sdma(wblk[:], wt_r[:, :, hcki*512:(hcki+1)*512].rearrange("n p h -> p n h"))
        for i in range(NT):
            for kt2 in range(NC):
                p.s("tensor", lambda e, kt2=kt2, i=i: e.matmul(psM[:], xc[:, kt2, i*128:(i+1)*128], wblk[:, kt2, :], start=(kt2 == 0), stop=(kt2 == NC - 1)))
            p.s("vector", lambda e, i=i: e.tensor_scalar(ev[:], psM[:], fct[:, i:i+1], None, ALU.mult))
            p.s("scalar", lambda e: e.activation(ev[:], ev[:], AF.Silu))
            p.sdma(h1_t[i][:, hcki*512:(hcki+1)*512], ev[:])
    cx.pop_scope()


def emit_bitffn_p2(cx, cm, aps, x5_dram, h1_dram, w2t_dram, x6_out, H=8192):
    p = cx.p
    NH = H // 128
    cx.push_scope()
    x_t = x5_dram.rearrange("(n p) c -> n p c", p=128)
    h_t = h1_dram.rearrange("(n p) h -> n p h", p=128)
    w2_r = aps["W2T"].rearrange("(n p) c -> n p c", p=128)
    wt2_r = w2t_dram.rearrange("(n p) c -> n p c", p=128)
    x6_t = x6_out.rearrange("(n p) c -> n p c", p=128)
    QT = 2
    WCH = 2048
    ldw = cx.sbuf("b2_ldw", [128, C], F32)
    zb = cx.sbuf("b2_zb", [128, WCH], F32)
    mb = cx.sbuf("b2_mb", [128, WCH], F32)
    oib = cx.sbuf("b2_oib", [128, WCH], I32)
    wtb = cx.sbuf("b2_wtb", [128, C], BF16)
    acc = cx.sbuf("b2_acc", [128, 2], F32)
    onescol = cx.sbuf("b2_onesc", [128, 1], F32)
    onesrow = cx.sbuf("b2_onesr", [1, 128], F32)
    sw2 = cx.sbuf("b2_sw2", [1, 2], F32)
    swv = cx.sbuf("b2_swv", [128, 1], F32)
    mnv = cx.sbuf("b2_mnv", [128, 1], F32)
    psS1 = cx.psum("b2_psS1", [1, 1], F32)
    psRep = cx.psum("b2_psRep", [128, 1], F32)
    ldh = cx.sbuf("b2_ldh", [128, H], F32)
    xqb = cx.sbuf("b2_xqb", [128, H], BF16)
    xc2 = cx.sbuf("b2_xc2", [128, NH, QT * 128], BF16)
    fct = cx.sbuf("b2_fct", [128, QT], F32)
    st = cx.sbuf("b2_st", [128, 8], F32)
    psT = cx.psum("b2_psT", [128, 512], BF16)
    wblk = cx.sbuf("b2_wblk", [128, 512], BF16)
    psM = [cx.psum(f"b2_psM{t}", [128, 512], F32) for t in range(QT)]
    obuf = cx.sbuf("b2_obuf", [128, QT, C], F32)
    xs = cx.sbuf("b2_xs", [128, C], F32)
    sq = cx.sbuf("b2_sq", [128, C], F32)
    stl = cx.sbuf("b2_stl", [128, 8], F32)

    p.s("vector", lambda e: e.memset(onescol[:], 1.0))
    p.s("vector", lambda e: e.memset(onesrow[:], 1.0))
    p.s("vector", lambda e: e.memset(acc[:, 0:1], 0.0))
    for ht in range(NH):
        p.sdma(ldw[:], w2_r[ht])
        p.s("vector", lambda e: e.reduce_sum(acc[:, 1:2], ldw[:], axis=AX.X, apply_absolute_value=True))
        p.s("vector", lambda e: e.tensor_tensor(acc[:, 0:1], acc[:, 0:1], acc[:, 1:2], ALU.add))
    p.s("tensor", lambda e: e.matmul(psS1[:], acc[:, 0:1], onescol[:], start=True, stop=True))
    p.s("vector", lambda e: e.tensor_scalar(sw2[:, 0:1], psS1[:], 1.0 / (C * H), 1e-5, ALU.mult, ALU.max))
    p.s("vector", lambda e: e.reciprocal(sw2[:, 1:2], sw2[:, 0:1]))
    p.s("tensor", lambda e: e.matmul(psRep[:], onesrow[:], sw2[:, 1:2], start=True, stop=True))
    p.s("scalar", lambda e: e.activation(swv[:], psRep[:], AF.Copy))
    p.s("tensor", lambda e: e.matmul(psRep[:], onesrow[:], sw2[:, 0:1], start=True, stop=True))
    p.s("scalar", lambda e: e.activation(mnv[:], psRep[:], AF.Copy))
    for ht in range(NH):
        p.sdma(ldw[:], w2_r[ht])
        emit_wquant(p, wtb, ldw, swv[:, 0:1], zb[:, 0:C], mb[:, 0:C], oib[:, 0:C])
        p.sdma(wt2_r[ht], wtb[:])
    for qu in range(NT // QT):
        for t in range(QT):
            i = qu * QT + t
            p.sdma(ldh[:], h_t[i])
            p.s("vector", lambda e: e.reduce_max(st[:, 0:1], ldh[:], axis=AX.X, apply_absolute_value=True))
            p.s("vector", lambda e: e.tensor_scalar(st[:, 0:1], st[:, 0:1], 1e-5, None, ALU.max))
            p.s("vector", lambda e: e.reciprocal(st[:, 1:2], st[:, 0:1]))
            p.s("vector", lambda e: e.tensor_scalar(st[:, 2:3], st[:, 1:2], 127.0, None, ALU.mult))
            for hch in range(H // WCH):
                sl2 = slice(hch*WCH, (hch+1)*WCH)
                emit_quant(p, xqb[:, sl2], ldh[:, sl2], st[:, 2:3], zb, mb, oib)
            p.s("vector", lambda e, t=t: e.tensor_scalar(fct[:, t:t+1], st[:, 0:1], 1.0 / 127.0, None, ALU.mult))
            p.s("vector", lambda e, t=t: e.tensor_tensor(fct[:, t:t+1], fct[:, t:t+1], mnv[:], ALU.mult))
            for c4 in range(NH // 4):
                for cc in range(4):
                    c = c4 * 4 + cc
                    p.s("tensor", lambda e, c=c, cc=cc: e.transpose(psT[:, ts(cc)], xqb[:, ts(c)], cm.idb[:]))
                p.s("scalar", lambda e, c4=c4, t=t: e.activation(
                    xc2[:, c4*4:(c4+1)*4, t*128:(t+1)*128],
                    psT[:].rearrange("p (a b) -> p a b", a=4), AF.Copy))
        for cck in range(C // 512):
            for kt2 in range(NH):
                p.sdma(wblk[:], wt2_r[kt2][:, cck*512:(cck+1)*512])
                for t in range(QT):
                    p.s("tensor", lambda e, kt2=kt2, t=t: e.matmul(psM[t][:], xc2[:, kt2, t*128:(t+1)*128], wblk[:], start=(kt2 == 0), stop=(kt2 == NH - 1), skip_group_check=True))
            for t in range(QT):
                p.s("vector", lambda e, t=t, cck=cck: e.tensor_scalar(obuf[:, t, cck*512:(cck+1)*512], psM[t][:], fct[:, t:t+1], None, ALU.mult))
        for t in range(QT):
            i = qu * QT + t
            p.sdma(xs[:], x_t[i])
            p.s("vector", lambda e, t=t: e.tensor_tensor(xs[:], xs[:], obuf[:, t, :], ALU.add))
            emit_ln_stats_tile(p, xs, sq, stl)
            p.s("vector", lambda e: e.tensor_scalar(xs[:], xs[:], stl[:, 2:3], stl[:, 7:8], ALU.subtract, ALU.mult))
            p.sdma(x6_t[i], xs[:])
    cx.pop_scope()


def build_fused(H=8192, I=5632):
    nc = bass.Bass("TRN2", target_bir_lowering=False, debug=False)
    ap = {}
    ins_spec = [
        ("x", [TT, C]), ("ident", [128, 128]), ("pv", [1, 2]), ("zmask", [1, NT]),
        ("ln1w", [128, C]), ("ln1b", [128, C]), ("ln2w", [128, C]), ("ln2b", [128, C]),
        ("tmk", [128, NC]), ("tmv", [128, NC]), ("tmr", [128, NC]),
        ("WkT", [C, C]), ("WvT", [C, C]), ("WrT", [C, C]), ("WoT", [C, C]),
        ("td8", [128, 8]), ("tf8", [128, 8]),
        ("cmk", [128, NC]), ("cmr", [128, NC]),
        ("WckT", [C, H]), ("WcvT", [H, C]), ("WcrT", [C, C]),
        ("gateT", [C, I]), ("upT", [C, I]), ("downT", [I, C]),
        ("gs", [128, I // 128]), ("us", [128, I // 128]), ("ds", [128, NC]),
        ("Wq1", [C, C]), ("Wk1", [C, C]), ("Wv1", [C, C]), ("Wo1", [C, C]),
        ("m1", [8, 128, 512]), ("m2", [8, 128, 512]), ("Sel", [32, NT]), ("cnt", [1, NT]),
        ("W1T", [C, H]), ("W2T", [H, C]),
    ]
    for n, sh in ins_spec:
        ap[n] = nc.dram_tensor(n, sh, F32, kind="ExternalInput").ap()
    out = nc.dram_tensor("out", [TT, C], F32, kind="ExternalOutput").ap()
    dr = {}
    for n, sh, dt in [
        ("xn", [TT, C], F32), ("halo1_in", [16, C], F32), ("halo1_out", [32, C], F32),
        ("k_c", [C, TT], F32), ("v_c", [C, TT], F32), ("sr_c", [C, TT], F32),
        ("kga", [2 * C, TT], F32), ("vga", [2 * C, TT], F32),
        ("y_my", [1024, 2 * TT], F32), ("yga", [C, 2 * TT], F32),
        ("x2", [TT, C], F32), ("xn2", [TT, C], F32),
        ("halo2_in", [16, C], F32), ("halo2_out", [32, C], F32),
        ("rkv_t", [TT, C], F32), ("x3", [TT, C], F32), ("ffo_t", [TT, C], F32),
        ("x4", [TT, C], F32), ("Q_c", [C, TT], F32),
        ("K_in", [C, TT], F32), ("V_in", [C, TT], F32),
        ("KG", [2 * C, TT], F32), ("VG", [2 * C, TT], F32),
        ("O_c", [C, TT], F32), ("x5pre", [TT, C], F32), ("x5", [TT, C], F32),
        ("h1", [TT, H], F32), ("w1t", [C, H], BF16), ("w2t", [H, C], BF16),
    ]:
        dr[n] = nc.dram_tensor("i_" + n, sh, dt).ap()

    cx = Ctx(nc)
    p = cx.p
    ccsem_cm = nc.semaphore("ccsem")
    ccsem = ccsem_cm.__enter__()
    ccount = [0]
    cm = Common(cx, ap["ident"])
    pvs = cx.sbuf("cm_pv", [1, 2], F32)
    zms = cx.sbuf("cm_zmask", [1, NT], F32)
    onesr_f = cx.sbuf("cm_onesrf", [1, 128], F32)
    pv128 = cx.sbuf("cm_pv128", [128, 2], F32)
    p.sdma(pvs[:], ap["pv"])
    p.sdma(zms[:], ap["zmask"])
    cx.push_scope()
    psPv = cx.psum("cm_psPv", [128, 2], F32)
    p.s("vector", lambda e: e.memset(onesr_f[:], 1.0))
    p.s("tensor", lambda e: e.matmul(psPv[:], onesr_f[:], pvs[:], start=True, stop=True))
    p.s("scalar", lambda e: e.activation(pv128[:], psPv[:], AF.Copy))
    cx.pop_scope()
    sv_halo = pvs[0:1, 0:1]
    sv_par = pv128[:, 1:2]

    emit_ln(cx, "ln1_", ap["x"], dr["xn"], ap["ln1w"], ap["ln1b"], halo_out=dr["halo1_in"])
    emit_cc(cx, "AllGather", dr["halo1_in"], dr["halo1_out"], ccsem, ccount)
    halo1 = dr["halo1_out"].rearrange("(two n) c -> two n c", two=2)
    emit_mix_kvr(cx, cm, ap, dr["xn"], halo1, sv_halo, zms, dr["k_c"], dr["v_c"], dr["sr_c"])
    emit_cc(cx, "AllGather", dr["k_c"], dr["kga"], ccsem, ccount)
    emit_cc(cx, "AllGather", dr["v_c"], dr["vga"], ccsem, ccount)
    emit_wkv(cx, ap, dr["kga"], dr["vga"], dr["y_my"], sv_par)
    emit_cc(cx, "AllGather", dr["y_my"], dr["yga"], ccsem, ccount)
    emit_wo_res(cx, cm, ap["WoT"], dr["yga"], dr["sr_c"], ap["x"], dr["x2"],
                use_sr=True, att_is_gather=True, svec_sb=sv_par)
    emit_ln(cx, "ln2_", dr["x2"], dr["xn2"], ap["ln2w"], ap["ln2b"], halo_out=dr["halo2_in"])
    emit_cc(cx, "AllGather", dr["halo2_in"], dr["halo2_out"], ccsem, ccount)
    halo2 = dr["halo2_out"].rearrange("(two n) c -> two n c", two=2)
    emit_cmix(cx, cm, ap, dr["xn2"], halo2, sv_halo, zms, dr["rkv_t"], H=H)
    emit_ln(cx, "ln3_", ap["x"], dr["x3"], r_ts=[dr["x2"], dr["rkv_t"]])
    emit_l0ffn(cx, cm, ap, dr["x3"], dr["ffo_t"], I=I)
    emit_ln(cx, "ln4_", dr["x3"], dr["x4"], r_ts=[dr["ffo_t"]])
    emit_proj3(cx, cm, ap, dr["x4"], dr["Q_c"], dr["K_in"], dr["V_in"])
    emit_cc(cx, "AllGather", dr["K_in"], dr["KG"], ccsem, ccount)
    emit_cc(cx, "AllGather", dr["V_in"], dr["VG"], ccsem, ccount)
    emit_attn(cx, cm, ap, dr["Q_c"], dr["KG"], dr["VG"], dr["O_c"])
    emit_wo_res(cx, cm, ap["Wo1"], dr["O_c"], None, dr["x4"], dr["x5pre"], use_sr=False, pfx="wo2_")
    emit_ln(cx, "ln5_", dr["x5pre"], dr["x5"])
    emit_bitffn(cx, cm, ap, dr["x5"], dr["h1"], dr["w1t"], dr["w2t"], out, H=H)
    nc_ret = cx.close()
    ccsem_cm.__exit__(None, None, None)
    return nc_ret


import concourse.bass_utils as bass_utils


def shard_tokens(x):
    B, T, F = x.shape
    xb = x.reshape(B, 32, 128, F)
    out = []
    for p2 in range(B):
        for r in range(2):
            out.append(np.ascontiguousarray(xb[p2, r::2].reshape(2048, F)))
    return out


def unshard_tokens(shards):
    F = shards[0].shape[-1]
    out = np.empty((4, 32, 128, F), np.float32)
    for p2 in range(4):
        for r in range(2):
            out[p2, r::2] = np.asarray(shards[2 * p2 + r], dtype=np.float32).reshape(16, 128, F)
    return out.reshape(4, 4096, F)


def rep128(v):
    return np.ascontiguousarray(np.broadcast_to(np.asarray(v, np.float32)[None, :], (128, len(v))))


def cvec(v, n=16):
    return np.ascontiguousarray(np.asarray(v, np.float32).reshape(n, 128).T)


_PROGS = {}


def kernel(**ins):
    T_ = lambda a: np.ascontiguousarray(np.asarray(a, np.float32).T)
    A_ = lambda a: np.ascontiguousarray(np.asarray(a, np.float32))
    ident = np.eye(128, dtype=np.float32)
    x_s = shard_tokens(np.asarray(ins["L0_attn_x"], np.float32))
    tri = (np.arange(128)[:, None] <= np.arange(128)[None, :]).astype(np.float32)

    def masks(rr):
        m1 = np.zeros((8, 128, 512), np.float32)
        m2 = np.zeros((8, 128, 512), np.float32)
        for mi in range(8):
            for t in range(4):
                g2 = 2 * t + rr
                if mi < g2:
                    m1[mi, :, t*128:(t+1)*128] = 1.0
                    m2[mi, :, t*128:(t+1)*128] = 1.0
                elif mi == g2:
                    m1[mi, :, t*128:(t+1)*128] = tri
                    m2[mi, :, t*128:(t+1)*128] = 1.0
        return m1, m2

    def sel_cnt(rr):
        Sel = np.zeros((32, 16), np.float32)
        cnt = np.zeros((1, 16), np.float32)
        for qj in range(16):
            g = 2 * qj + rr
            Sel[g+1:, qj] = 1.0
            cnt[0, qj] = 4096 - (g + 1) * 128
        return Sel, cnt

    in_maps = []
    for i in range(8):
        r = i % 2
        m1, m2 = masks(r)
        Sel, cnt = sel_cnt(r)
        zmask = np.ones((1, 16), np.float32)
        if r == 0:
            zmask[0, 0] = 0.0
        sl = slice(r * 1024, (r + 1) * 1024)
        in_maps.append({
            "x": x_s[i], "ident": ident, "pv": np.array([[1.0 - r, float(r)]], np.float32), "zmask": zmask,
            "ln1w": rep128(ins["L0_attn_ln1_w"]), "ln1b": rep128(ins["L0_attn_ln1_b"]),
            "ln2w": rep128(ins["L0_attn_ln2_w"]), "ln2b": rep128(ins["L0_attn_ln2_b"]),
            "tmk": cvec(ins["L0_attn_tmk"]), "tmv": cvec(ins["L0_attn_tmv"]), "tmr": cvec(ins["L0_attn_tmr"]),
            "WkT": T_(ins["L0_attn_Wk"]), "WvT": T_(ins["L0_attn_Wv"]), "WrT": T_(ins["L0_attn_Wr"]),
            "WoT": T_(ins["L0_attn_Wo"]),
            "td8": cvec(ins["L0_attn_time_decay"][sl], 8), "tf8": cvec(ins["L0_attn_time_first"][sl], 8),
            "cmk": cvec(ins["L0_attn_cmk"]), "cmr": cvec(ins["L0_attn_cmr"]),
            "WckT": T_(ins["L0_attn_Wck"]), "WcvT": T_(ins["L0_attn_Wcv"]), "WcrT": T_(ins["L0_attn_Wcr"]),
            "gateT": T_(ins["L0_ffn_gate_w"]), "upT": T_(ins["L0_ffn_up_w"]), "downT": T_(ins["L0_ffn_down_w"]),
            "gs": cvec(ins["L0_ffn_gate_s"], 44), "us": cvec(ins["L0_ffn_up_s"], 44), "ds": cvec(ins["L0_ffn_down_s"], 16),
            "Wq1": A_(ins["L1_attn_Wq"]), "Wk1": A_(ins["L1_attn_Wk"]), "Wv1": A_(ins["L1_attn_Wv"]),
            "Wo1": A_(ins["L1_attn_Wo"]),
            "m1": m1, "m2": m2, "Sel": Sel, "cnt": cnt,
            "W1T": T_(ins["L1_ffn_W1"]), "W2T": T_(ins["L1_ffn_W2"]),
        })
    if "nc" not in _PROGS:
        _PROGS["nc"] = build_fused()
    res = bass_utils.run_bass_kernel_spmd(_PROGS["nc"], in_maps, core_ids=list(range(8)))
    return unshard_tokens([res.results[i]["out"] for i in range(8)])
